# version 01
# baseline (speedup 1.0000x reference)
import numpy as np
import ml_dtypes
import concourse.bass as bass
import concourse.mybir as mybir
from concourse.bass_utils import run_bass_kernel_spmd

F32 = mybir.dt.float32
BF16 = mybir.dt.bfloat16
I32 = mybir.dt.int32
AF = mybir.ActivationFunctionType
ALU = mybir.AluOpType
AX = mybir.AxisListType
BF = ml_dtypes.bfloat16

NCORE = 8
D = 4096
SEQ = 2048
B = 4
NTOK = B * SEQ
TPC = NTOK // NCORE
H1 = 11008
HC1 = H1 // 128
EPS = 1e-5
NEG = -1.0e9


class Ser:

    def __init__(self, nc, name="ser"):
        self.nc = nc
        self._cm = nc.semaphore(name)
        self.sem = self._cm.__enter__()
        self.v = 0

    def do(self, eng, fn, inc=1):
        eng.wait_ge(self.sem, self.v)
        inst = fn()
        inst.then_inc(self.sem, inc)
        self.v += inc
        return inst

    def chain(self, eng, fns, inc=1):
        eng.wait_ge(self.sem, self.v)
        for f in fns[:-1]:
            f()
        fns[-1]().then_inc(self.sem, inc)
        self.v += inc

    def fence(self, eng):
        eng.wait_ge(self.sem, self.v)


def make_identity(nc, S, ctx):
    iot = ctx(nc.sbuf_tensor("idn_iot", [128, 128], I32))
    iotc = ctx(nc.sbuf_tensor("idn_iotc", [128, 1], I32))
    iotf = ctx(nc.sbuf_tensor("idn_iotf", [128, 128], F32))
    iotcf = ctx(nc.sbuf_tensor("idn_iotcf", [128, 1], F32))
    idn = ctx(nc.sbuf_tensor("idn", [128, 128], BF16))
    S.do(nc.gpsimd, lambda: nc.gpsimd.iota(iot[:], pattern=[[1, 128]], channel_multiplier=0))
    S.do(nc.gpsimd, lambda: nc.gpsimd.iota(iotc[:], pattern=[[0, 1]], channel_multiplier=1))
    S.do(nc.vector, lambda: nc.vector.tensor_copy(iotf[:], iot[:]))
    S.do(nc.vector, lambda: nc.vector.tensor_copy(iotcf[:], iotc[:]))
    S.do(nc.vector, lambda: nc.vector.tensor_single_scalar(idn[:], iotf[:], iotcf[:], ALU.is_equal))
    return idn


def emit_ln_transpose(nc, S, x_dram, row0, ntile, xn_fm, col0, idn, ctx_tensors):
    xt = ctx_tensors["ln_xt"]
    junk = ctx_tensors["ln_junk"]
    xn_tm = ctx_tensors["ln_xn_tm"]
    s1 = ctx_tensors["ln_s1"]
    ssq = ctx_tensors["ln_ssq"]
    m = ctx_tensors["ln_m"]
    var = ctx_tensors["ln_var"]
    t0 = ctx_tensors["ln_t0"]
    rstd = ctx_tensors["ln_rstd"]
    nmr = ctx_tensors["ln_nmr"]
    pst = ctx_tensors["ln_pst"]
    FM_W = xn_fm.shape[1] // 32

    for t in range(ntile):
        S.do(nc.sync, lambda t=t: nc.sync.dma_start(
            xt[:], x_dram[row0 + 128 * t: row0 + 128 * (t + 1), :]), 16)
        S.do(nc.vector, lambda: nc.vector.reduce_sum(s1[:], xt[:], axis=AX.X))
        S.do(nc.scalar, lambda: nc.scalar.activation(junk[:], xt[:], AF.Square, accum_out=ssq[:]))
        S.do(nc.vector, lambda: nc.vector.tensor_single_scalar(m[:], s1[:], 1.0 / D, ALU.mult))
        S.do(nc.vector, lambda: nc.vector.tensor_single_scalar(var[:], ssq[:], 1.0 / D, ALU.mult))
        S.do(nc.vector, lambda: nc.vector.tensor_mul(t0[:], m[:], m[:]))
        S.do(nc.vector, lambda: nc.vector.tensor_sub(var[:], var[:], t0[:]))
        S.do(nc.vector, lambda: nc.vector.tensor_single_scalar(var[:], var[:], EPS, ALU.add))
        S.do(nc.scalar, lambda: nc.scalar.activation(t0[:], var[:], AF.Ln))
        S.do(nc.scalar, lambda: nc.scalar.activation(rstd[:], t0[:], AF.Exp, scale=-0.5))
        S.do(nc.vector, lambda: nc.vector.tensor_mul(nmr[:], m[:], rstd[:]))
        S.do(nc.vector, lambda: nc.vector.tensor_single_scalar(nmr[:], nmr[:], -1.0, ALU.mult))
        S.do(nc.scalar, lambda: nc.scalar.activation(xn_tm[:], xt[:], AF.Identity,
                                                     bias=nmr[:], scale=rstd[:]))
        for dc in range(32):
            S.do(nc.tensor, lambda dc=dc: nc.tensor.transpose(
                pst[:], xn_tm[:, 128 * dc:128 * (dc + 1)], idn[:]))
            S.do(nc.vector, lambda dc=dc, t=t: nc.vector.tensor_copy(
                xn_fm[:, dc * FM_W + col0 + 128 * t: dc * FM_W + col0 + 128 * (t + 1)], pst[:]))


def alloc_ln_scratch(nc, ctx, tag=""):
    d = {}
    d["ln_xt"] = ctx(nc.sbuf_tensor("ln_xt" + tag, [128, 4096], F32))
    d["ln_junk"] = ctx(nc.sbuf_tensor("ln_junk" + tag, [128, 4096], BF16))
    d["ln_xn_tm"] = ctx(nc.sbuf_tensor("ln_xn_tm" + tag, [128, 4096], BF16))
    for nm in ["ln_s1", "ln_ssq", "ln_m", "ln_var", "ln_t0", "ln_rstd", "ln_nmr"]:
        d[nm] = ctx(nc.sbuf_tensor(nm + tag, [128, 1], F32))
    d["ln_pst"] = ctx(nc.psum_tensor("ln_pst" + tag, [128, 128], BF16))
    return d


def build_ffn0():
    nc = bass.Bass("TRN2", target_bir_lowering=False)
    x = nc.dram_tensor("x", [TPC, D], F32, kind="ExternalInput").ap()
    bwT = nc.dram_tensor("bwT", [D, D], F32, kind="ExternalInput").ap()
    gwT = nc.dram_tensor("gwT", [D, 28], F32, kind="ExternalInput").ap()
    aaT = nc.dram_tensor("aaT", [D, 224], F32, kind="ExternalInput").ap()
    bal = nc.dram_tensor("bal", [224, D], F32, kind="ExternalInput").ap()
    erep = nc.dram_tensor("erep", [28, 224], F32, kind="ExternalInput").ap()
    xout = nc.dram_tensor("xout", [TPC, D], F32, kind="ExternalOutput").ap()

    from contextlib import ExitStack
    with ExitStack() as st:
        ctx = st.enter_context
        S = Ser(nc)
        st.enter_context(nc.Block())
        idn = make_identity(nc, S, ctx)
        xn = ctx(nc.sbuf_tensor("xn", [128, 32 * TPC], BF16))
        with ExitStack() as stA:
            lns = alloc_ln_scratch(nc, stA.enter_context)
            emit_ln_transpose(nc, S, x, 0, TPC // 128, xn, 0, idn, lns)

        gwf = ctx(nc.sbuf_tensor("gwf", [128, 28], F32))
        gwb = ctx(nc.sbuf_tensor("gwb", [128, 32 * 28], BF16))
        lg_fm = ctx(nc.sbuf_tensor("lg_fm", [28, TPC], BF16))
        lg_tm = ctx(nc.sbuf_tensor("lg_tm", [128, 8 * 28], F32))
        g_fm = ctx(nc.sbuf_tensor("g_fm", [28, TPC], BF16))
        grep = ctx(nc.sbuf_tensor("grep", [128, 2 * TPC], F32))
        h_fm = ctx(nc.sbuf_tensor("h_fm", [128, 2 * TPC], BF16))
        stBC = st.enter_context(ExitStack())
        ctxB = stBC.enter_context
        ps_g = ctxB(nc.psum_tensor("ps_g", [28, 512], F32))
        ps_l1 = ctxB(nc.psum_tensor("ps_l1", [128, 28], BF16))
        ps_l2 = ctxB(nc.psum_tensor("ps_l2", [28, 128], BF16))
        ps_r = ctxB(nc.psum_tensor("ps_r", [128, 512], F32))
        for dc in range(32):
            S.do(nc.sync, lambda dc=dc: nc.sync.dma_start(
                gwf[:], gwT[128 * dc:128 * (dc + 1), :]), 16)
            S.do(nc.scalar, lambda dc=dc: nc.scalar.copy(gwb[:, 28 * dc:28 * (dc + 1)], gwf[:]))
        for th in range(2):
            S.chain(nc.tensor, [
                (lambda dc=dc, th=th: nc.tensor.matmul(
                    ps_g[:], gwb[:, 28 * dc:28 * (dc + 1)],
                    xn[:, dc * TPC + th * 512: dc * TPC + th * 512 + 512],
                    start=(dc == 0), stop=(dc == 31)))
                for dc in range(32)])
            S.do(nc.vector, lambda th=th: nc.vector.tensor_copy(
                lg_fm[:, th * 512:(th + 1) * 512], ps_g[:]))
        for tt in range(8):
            S.do(nc.tensor, lambda tt=tt: nc.tensor.transpose(
                ps_l1[:], lg_fm[:, 128 * tt:128 * (tt + 1)], idn[0:28, 0:28]))
            S.do(nc.vector, lambda tt=tt: nc.vector.tensor_copy(
                lg_tm[:, 28 * tt:28 * (tt + 1)], ps_l1[:]))
        top8 = ctx(nc.sbuf_tensor("top8", [128, 8], F32))
        nm1 = ctx(nc.sbuf_tensor("nm1", [128, 1], F32))
        em2 = ctx(nc.sbuf_tensor("em2", [128, 1], F32))
        rec2 = ctx(nc.sbuf_tensor("rec2", [128, 1], F32))
        e_t = ctx(nc.sbuf_tensor("e_t", [128, 28], F32))
        msk = ctx(nc.sbuf_tensor("msk", [128, 28], F32))
        g_tm = ctx(nc.sbuf_tensor("g_tm", [128, 8 * 28], BF16))
        for tt in range(8):
            lg = lg_tm[:, 28 * tt:28 * (tt + 1)]
            S.do(nc.vector, lambda lg=lg: nc.vector.max(out=top8[:], in_=lg))
            S.do(nc.vector, lambda: nc.vector.tensor_single_scalar(nm1[:], top8[:, 0:1], -1.0, ALU.mult))
            S.do(nc.vector, lambda: nc.vector.tensor_sub(em2[:], top8[:, 1:2], top8[:, 0:1]))
            S.do(nc.scalar, lambda: nc.scalar.activation(em2[:], em2[:], AF.Exp))
            S.do(nc.vector, lambda: nc.vector.tensor_single_scalar(em2[:], em2[:], 1.0, ALU.add))
            S.do(nc.vector, lambda: nc.vector.reciprocal(rec2[:], em2[:]))
            S.do(nc.vector, lambda: nc.vector.tensor_single_scalar(rec2[:], rec2[:], 2.0, ALU.mult))
            S.do(nc.scalar, lambda lg=lg: nc.scalar.activation(e_t[:], lg, AF.Exp, bias=nm1[:]))
            S.do(nc.vector, lambda lg=lg: nc.vector.tensor_single_scalar(msk[:], lg, top8[:, 1:2], ALU.is_ge))
            S.do(nc.vector, lambda: nc.vector.tensor_mul(e_t[:], e_t[:], msk[:]))
            S.do(nc.vector, lambda tt=tt: nc.vector.tensor_single_scalar(
                g_tm[:, 28 * tt:28 * (tt + 1)], e_t[:], rec2[:], ALU.mult))
        for tt in range(8):
            S.do(nc.tensor, lambda tt=tt: nc.tensor.transpose(
                ps_l2[:], g_tm[:, 28 * tt:28 * (tt + 1)], idn[:]))
            S.do(nc.vector, lambda tt=tt: nc.vector.tensor_copy(
                g_fm[:, 128 * tt:128 * (tt + 1)], ps_l2[:]))
        erf = ctxB(nc.sbuf_tensor("erf", [28, 224], F32))
        erb = ctxB(nc.sbuf_tensor("erb", [28, 224], BF16))
        S.do(nc.sync, lambda: nc.sync.dma_start(erf[:], erep), 16)
        S.do(nc.scalar, lambda: nc.scalar.copy(erb[:], erf[:]))
        for mc, mw in ((0, 128), (1, 96)):
            for th in range(2):
                S.chain(nc.tensor, [lambda mc=mc, mw=mw, th=th: nc.tensor.matmul(
                    ps_r[0:mw, :], erb[:, 128 * mc:128 * mc + mw],
                    g_fm[:, th * 512:(th + 1) * 512], start=True, stop=True)])
                S.do(nc.vector, lambda mc=mc, mw=mw, th=th: nc.vector.tensor_copy(
                    grep[0:mw, mc * TPC + th * 512: mc * TPC + (th + 1) * 512], ps_r[0:mw, :]))

        aaf = ctxB(nc.sbuf_tensor("aaf", [128, 224], F32))
        aab = ctxB(nc.sbuf_tensor("aab", [128, 32 * 224], BF16))
        for dc in range(32):
            S.do(nc.sync, lambda dc=dc: nc.sync.dma_start(
                aaf[:], aaT[128 * dc:128 * (dc + 1), :]), 16)
            S.do(nc.scalar, lambda dc=dc: nc.scalar.copy(aab[:, 224 * dc:224 * (dc + 1)], aaf[:]))
        for mc, mw in ((0, 128), (1, 96)):
            for th in range(2):
                S.chain(nc.tensor, [
                    (lambda dc=dc, mc=mc, mw=mw, th=th: nc.tensor.matmul(
                        ps_r[0:mw, :], aab[:, 224 * dc + 128 * mc: 224 * dc + 128 * mc + mw],
                        xn[:, dc * TPC + th * 512: dc * TPC + th * 512 + 512],
                        start=(dc == 0), stop=(dc == 31)))
                    for dc in range(32)])
                S.do(nc.vector, lambda mc=mc, mw=mw, th=th: nc.vector.tensor_mul(
                    h_fm[0:mw, mc * TPC + th * 512: mc * TPC + (th + 1) * 512],
                    ps_r[0:mw, :], grep[0:mw, mc * TPC + th * 512: mc * TPC + (th + 1) * 512]))

        stBC.close()
        wf = ctx(nc.sbuf_tensor("wf", [128, 512], F32))
        wb = ctx(nc.sbuf_tensor("wb", [128, 512], BF16))
        bbf = ctx(nc.sbuf_tensor("bbf", [128, 512], F32))
        bb0 = ctx(nc.sbuf_tensor("bb0", [128, 512], BF16))
        bb1 = ctx(nc.sbuf_tensor("bb1", [96, 512], BF16))
        xr = ctx(nc.sbuf_tensor("xr", [128, 512], F32))
        o_sb = ctx(nc.sbuf_tensor("o_sb", [128, 512], F32))
        pso = [ctx(nc.psum_tensor(f"pso{i}", [128, 512], F32)) for i in range(8)]
        for ob in range(8):
            S.do(nc.sync, lambda ob=ob: nc.sync.dma_start(bbf[:], bal[0:128, 512 * ob:512 * (ob + 1)]), 16)
            S.do(nc.scalar, lambda: nc.scalar.copy(bb0[:], bbf[:]))
            S.do(nc.sync, lambda ob=ob: nc.sync.dma_start(bbf[0:96, :], bal[128:224, 512 * ob:512 * (ob + 1)]), 16)
            S.do(nc.scalar, lambda: nc.scalar.copy(bb1[:], bbf[0:96, :]))
            for dc in range(32):
                S.do(nc.sync, lambda dc=dc, ob=ob: nc.sync.dma_start(
                    wf[:], bwT[128 * dc:128 * (dc + 1), 512 * ob:512 * (ob + 1)]), 16)
                S.do(nc.scalar, lambda: nc.scalar.copy(wb[:], wf[:]))
                S.chain(nc.tensor, [
                    (lambda tt=tt, dc=dc: nc.tensor.matmul(
                        pso[tt][:], xn[:, dc * TPC + 128 * tt: dc * TPC + 128 * (tt + 1)],
                        wb[:], start=(dc == 0), stop=False, skip_group_check=True))
                    for tt in range(8)])
            S.chain(nc.tensor, [
                (lambda tt=tt: nc.tensor.matmul(
                    pso[tt][:], h_fm[:, 128 * tt:128 * (tt + 1)], bb0[:],
                    start=False, stop=False, skip_group_check=True))
                for tt in range(8)])
            S.chain(nc.tensor, [
                (lambda tt=tt: nc.tensor.matmul(
                    pso[tt][:], h_fm[0:96, TPC + 128 * tt: TPC + 128 * (tt + 1)], bb1[:],
                    start=False, stop=True, skip_group_check=True))
                for tt in range(8)])
            for tt in range(8):
                S.do(nc.sync, lambda tt=tt, ob=ob: nc.sync.dma_start(
                    xr[:], x[128 * tt:128 * (tt + 1), 512 * ob:512 * (ob + 1)]), 16)
                S.do(nc.vector, lambda tt=tt: nc.vector.scalar_tensor_tensor(
                    o_sb[:], pso[tt][:], 0.5, xr[:], ALU.mult, ALU.add))
                S.do(nc.sync, lambda tt=tt, ob=ob: nc.sync.dma_start(
                    xout[128 * tt:128 * (tt + 1), 512 * ob:512 * (ob + 1)], o_sb[:]), 16)
        S.fence(nc.sync)
    return nc


def build_attn(scale, tag):
    nc = bass.Bass("TRN2", target_bir_lowering=False)
    x = nc.dram_tensor("x", [SEQ, D], F32, kind="ExternalInput").ap()
    wqT = nc.dram_tensor("wqT", [D, 4096], F32, kind="ExternalInput").ap()
    wkT = nc.dram_tensor("wkT", [D, 1024], F32, kind="ExternalInput").ap()
    wvT = nc.dram_tensor("wvT", [D, 1024], F32, kind="ExternalInput").ap()
    woT = nc.dram_tensor("woT", [D, 4096], F32, kind="ExternalInput").ap()
    bq = nc.dram_tensor("bq", [128, 32], F32, kind="ExternalInput").ap()
    bk = nc.dram_tensor("bk", [128, 8], F32, kind="ExternalInput").ap()
    bv = nc.dram_tensor("bv", [128, 8], F32, kind="ExternalInput").ap()
    bo = nc.dram_tensor("bo", [4096], F32, kind="ExternalInput").ap()
    cosq = nc.dram_tensor("cosq", [128, TPC], F32, kind="ExternalInput").ap()
    sinq = nc.dram_tensor("sinq", [128, TPC], F32, kind="ExternalInput").ap()
    cosk = nc.dram_tensor("cosk", [128, SEQ], F32, kind="ExternalInput").ap()
    sink = nc.dram_tensor("sink", [128, SEQ], F32, kind="ExternalInput").ap()
    maskT = nc.dram_tensor("maskT", [SEQ, TPC], F32, kind="ExternalInput").ap()
    xout = nc.dram_tensor("xout", [TPC, D], F32, kind="ExternalOutput").ap()
    q_hbm = nc.dram_tensor("q_scr_" + tag, [4096, TPC], BF16, kind="Internal").ap()
    ctx_hbm = nc.dram_tensor("ctx_scr_" + tag, [4096, TPC], BF16, kind="Internal").ap()

    from contextlib import ExitStack
    with ExitStack() as st:
        ctx = st.enter_context
        S = Ser(nc)
        st.enter_context(nc.Block())
        idn = make_identity(nc, S, ctx)
        k_fm = ctx(nc.sbuf_tensor("k_fm", [128, 8 * SEQ], BF16))
        v_tm = ctx(nc.sbuf_tensor("v_tm", [128, 8 * SEQ], BF16))

        with ExitStack() as stA:
            ctxA = stA.enter_context
            lns = alloc_ln_scratch(nc, ctxA, "a")
            cos_k = ctxA(nc.sbuf_tensor("cos_k", [128, SEQ], F32))
            sin_k = ctxA(nc.sbuf_tensor("sin_k", [128, SEQ], F32))
            cos_q = ctxA(nc.sbuf_tensor("cos_q", [128, TPC], F32))
            sin_q = ctxA(nc.sbuf_tensor("sin_q", [128, TPC], F32))
            bsb = ctxA(nc.sbuf_tensor("bsb", [128, 48], F32))
            S.do(nc.sync, lambda: nc.sync.dma_start(cos_k[:], cosk), 16)
            S.do(nc.sync, lambda: nc.sync.dma_start(sin_k[:], sink), 16)
            S.do(nc.sync, lambda: nc.sync.dma_start(cos_q[:], cosq), 16)
            S.do(nc.sync, lambda: nc.sync.dma_start(sin_q[:], sinq), 16)
            S.do(nc.sync, lambda: nc.sync.dma_start(bsb[:, 0:32], bq), 16)
            S.do(nc.sync, lambda: nc.sync.dma_start(bsb[:, 32:40], bk), 16)
            S.do(nc.sync, lambda: nc.sync.dma_start(bsb[:, 40:48], bv), 16)
            xnb = ctxA(nc.sbuf_tensor("xnb", [128, 32 * 512], BF16))
            wf = ctxA(nc.sbuf_tensor("wf_a", [128, 128], F32))
            wbb = ctxA(nc.sbuf_tensor("wb_a", [128, 128], BF16))
            pkv = ctxA(nc.psum_tensor("pkv", [128, 512], F32))
            pvt = ctxA(nc.psum_tensor("pvt", [128, 128], BF16))
            kvf = ctxA(nc.sbuf_tensor("kvf", [128, 512], F32))
            rot = ctxA(nc.sbuf_tensor("rot", [128, 512], F32))
            t1 = ctxA(nc.sbuf_tensor("t1_a", [128, 512], F32))
            t2 = ctxA(nc.sbuf_tensor("t2_a", [128, 512], F32))
            vb = ctxA(nc.sbuf_tensor("vb_a", [128, 512], BF16))

            def rope_emit(cos_t, sin_t, dst_bf):
                S.do(nc.vector, lambda: nc.vector.tensor_single_scalar(rot[0:64, :], kvf[64:128, :], -1.0, ALU.mult))
                S.do(nc.vector, lambda: nc.vector.tensor_copy(rot[64:128, :], kvf[0:64, :]))
                S.do(nc.vector, lambda: nc.vector.tensor_mul(t1[:], kvf[:], cos_t))
                S.do(nc.vector, lambda: nc.vector.tensor_mul(t2[:], rot[:], sin_t))
                S.do(nc.vector, lambda: nc.vector.tensor_add(dst_bf, t1[:], t2[:]))

            for tg in range(4):
                emit_ln_transpose(nc, S, x, tg * 512, 4, xnb, 0, idn, lns)

                def proj(oc_w, w_dram, wcol, bias_col, out_emit):
                    for dc in range(32):
                        S.do(nc.sync, lambda dc=dc: nc.sync.dma_start(
                            wf[:], w_dram[128 * dc:128 * (dc + 1), 128 * wcol:128 * (wcol + 1)]), 16)
                        S.do(nc.scalar, lambda: nc.scalar.copy(wbb[:], wf[:]))
                        S.chain(nc.tensor, [lambda dc=dc: nc.tensor.matmul(
                            pkv[:], wbb[:], xnb[:, dc * 512:(dc + 1) * 512],
                            start=(dc == 0), stop=(dc == 31))])
                    S.do(nc.scalar, lambda bias_col=bias_col: nc.scalar.activation(
                        kvf[:], pkv[:], AF.Identity, bias=bsb[:, bias_col:bias_col + 1]))
                    out_emit()

                for g in range(8):
                    def out_k(g=g, tg=tg):
                        rope_emit(cos_k[:, tg * 512:(tg + 1) * 512], sin_k[:, tg * 512:(tg + 1) * 512],
                                  k_fm[:, g * SEQ + tg * 512: g * SEQ + (tg + 1) * 512])
                    proj(1, wkT, g, 32 + g, out_k)
                for g in range(8):
                    def out_v(g=g, tg=tg):
                        S.do(nc.scalar, lambda: nc.scalar.copy(vb[:], kvf[:]))
                        for i in range(4):
                            S.do(nc.tensor, lambda i=i: nc.tensor.transpose(
                                pvt[:], vb[:, 128 * i:128 * (i + 1)], idn[:]))
                            S.do(nc.vector, lambda i=i, g=g, tg=tg: nc.vector.tensor_copy(
                                v_tm[:, (g * 16 + tg * 4 + i) * 128:(g * 16 + tg * 4 + i) * 128 + 128],
                                pvt[:]))
                    proj(1, wvT, g, 40 + g, out_v)
                if tg < 2:
                    for h in range(32):
                        def out_q(h=h, tg=tg):
                            rope_emit(cos_q[:, tg * 512:(tg + 1) * 512],
                                      sin_q[:, tg * 512:(tg + 1) * 512], vb[:])
                            S.do(nc.sync, lambda h=h, tg=tg: nc.sync.dma_start(
                                q_hbm[128 * h:128 * (h + 1), tg * 512:(tg + 1) * 512], vb[:]), 16)
                        proj(1, wqT, h, h, out_q)

        with ExitStack() as stB:
            ctxB = stB.enter_context
            mk = ctxB(nc.sbuf_tensor("mk", [128, 16 * TPC], F32))
            for kt in range(16):
                S.do(nc.sync, lambda kt=kt: nc.sync.dma_start(
                    mk[:, kt * TPC:(kt + 1) * TPC], maskT[128 * kt:128 * (kt + 1), :]), 16)
            ones_c = ctxB(nc.sbuf_tensor("ones_c", [128, 1], BF16))
            ones_r = ctxB(nc.sbuf_tensor("ones_r", [1, 128], F32))
            S.do(nc.vector, lambda: nc.vector.memset(ones_c[:], 1.0))
            S.do(nc.vector, lambda: nc.vector.memset(ones_r[:], 1.0))
            qh_sb = ctxB(nc.sbuf_tensor("qh_sb", [128, TPC], BF16))
            p_T = ctxB(nc.sbuf_tensor("p_T", [128, 512], BF16))
            rec_d = ctxB(nc.sbuf_tensor("rec_d", [1, 512], F32))
            bc_sb = ctxB(nc.sbuf_tensor("bc_sb", [128, 512], F32))
            ctx_sb = ctxB(nc.sbuf_tensor("ctx_sb", [128, 512], BF16))
            ps_s = ctxB(nc.psum_tensor("ps_s", [128, 512], F32))
            ps_c = ctxB(nc.psum_tensor("ps_c", [128, 512], F32))
            ps_d = ctxB(nc.psum_tensor("ps_d", [1, 512], F32))
            ps_b = ctxB(nc.psum_tensor("ps_b", [128, 512], F32))
            for h in range(32):
                g = h // 4
                S.do(nc.sync, lambda h=h: nc.sync.dma_start(qh_sb[:], q_hbm[128 * h:128 * (h + 1), :]), 16)
                for qh in range(2):
                    for kt in range(16):
                        S.chain(nc.tensor, [lambda g=g, kt=kt, qh=qh: nc.tensor.matmul(
                            ps_s[:], k_fm[:, g * SEQ + kt * 128: g * SEQ + kt * 128 + 128],
                            qh_sb[:, qh * 512:(qh + 1) * 512], start=True, stop=True,
                            skip_group_check=True)])
                        S.do(nc.vector, lambda kt=kt, qh=qh: nc.vector.tensor_add(
                            ps_s[:], ps_s[:], mk[:, kt * TPC + qh * 512: kt * TPC + qh * 512 + 512]))
                        S.do(nc.scalar, lambda: nc.scalar.activation(p_T[:], ps_s[:], AF.Exp, scale=scale))
                        S.chain(nc.tensor, [lambda g=g, kt=kt: nc.tensor.matmul(
                            ps_c[:], v_tm[:, (g * 16 + kt) * 128:(g * 16 + kt) * 128 + 128], p_T[:],
                            start=(kt == 0), stop=(kt == 15), skip_group_check=True)])
                        S.chain(nc.tensor, [lambda kt=kt: nc.tensor.matmul(
                            ps_d[:], ones_c[:], p_T[:],
                            start=(kt == 0), stop=(kt == 15), skip_group_check=True)])
                    S.do(nc.vector, lambda: nc.vector.reciprocal(rec_d[:], ps_d[:]))
                    S.chain(nc.tensor, [lambda: nc.tensor.matmul(
                        ps_b[:], ones_r[:], rec_d[:], start=True, stop=True, skip_group_check=True)])
                    S.do(nc.vector, lambda: nc.vector.tensor_copy(bc_sb[:], ps_b[:]))
                    S.do(nc.vector, lambda: nc.vector.tensor_mul(ctx_sb[:], ps_c[:], bc_sb[:]))
                    S.do(nc.sync, lambda h=h, qh=qh: nc.sync.dma_start(
                        ctx_hbm[128 * h:128 * (h + 1), qh * 512:(qh + 1) * 512], ctx_sb[:]), 16)

        with ExitStack() as stC:
            ctxC = stC.enter_context
            cfm = ctxC(nc.sbuf_tensor("cfm", [128, 32 * TPC], BF16))
            for h in range(32):
                S.do(nc.sync, lambda h=h: nc.sync.dma_start(
                    cfm[:, h * TPC:(h + 1) * TPC], ctx_hbm[128 * h:128 * (h + 1), :]), 16)
            bo_sb = ctxC(nc.sbuf_tensor("bo_sb", [1, 4096], F32))
            bias_bc = ctxC(nc.sbuf_tensor("bias_bc", [128, 4096], F32))
            ones_r2 = ctxC(nc.sbuf_tensor("ones_r2", [1, 128], F32))
            wf = ctxC(nc.sbuf_tensor("wf_c", [128, 512], F32))
            wbb = ctxC(nc.sbuf_tensor("wb_c", [128, 512], BF16))
            xr = ctxC(nc.sbuf_tensor("xr_c", [128, 512], F32))
            o_sb = ctxC(nc.sbuf_tensor("o_sb_c", [128, 512], F32))
            pso = [ctxC(nc.psum_tensor(f"pso_c{i}", [128, 512], F32)) for i in range(8)]
            S.do(nc.vector, lambda: nc.vector.memset(ones_r2[:], 1.0))
            S.do(nc.sync, lambda: nc.sync.dma_start(bo_sb[:], bo[None, :]), 16)
            for ob in range(8):
                S.chain(nc.tensor, [lambda ob=ob: nc.tensor.matmul(
                    pso[0][:], ones_r2[:], bo_sb[:, 512 * ob:512 * (ob + 1)],
                    start=True, stop=True)])
                S.do(nc.vector, lambda ob=ob: nc.vector.tensor_copy(
                    bias_bc[:, 512 * ob:512 * (ob + 1)], pso[0][:]))
            for ob in range(8):
                for dc in range(32):
                    S.do(nc.sync, lambda dc=dc, ob=ob: nc.sync.dma_start(
                        wf[:], woT[128 * dc:128 * (dc + 1), 512 * ob:512 * (ob + 1)]), 16)
                    S.do(nc.scalar, lambda: nc.scalar.copy(wbb[:], wf[:]))
                    S.chain(nc.tensor, [
                        (lambda tt=tt, dc=dc: nc.tensor.matmul(
                            pso[tt][:], cfm[:, dc * TPC + 128 * tt: dc * TPC + 128 * (tt + 1)],
                            wbb[:], start=(dc == 0), stop=(dc == 31), skip_group_check=True))
                        for tt in range(8)])
                for tt in range(8):
                    S.do(nc.sync, lambda tt=tt, ob=ob: nc.sync.dma_start(
                        xr[:], x[128 * tt:128 * (tt + 1), 512 * ob:512 * (ob + 1)]), 16)
                    S.do(nc.vector, lambda tt=tt: nc.vector.scalar_tensor_tensor(
                        o_sb[:], pso[tt][:], 1.0, xr[:], ALU.mult, ALU.add))
                    S.do(nc.vector, lambda ob=ob: nc.vector.tensor_add(
                        o_sb[:], o_sb[:], bias_bc[:, 512 * ob:512 * (ob + 1)]))
                    S.do(nc.sync, lambda tt=tt, ob=ob: nc.sync.dma_start(
                        xout[128 * tt:128 * (tt + 1), 512 * ob:512 * (ob + 1)], o_sb[:]), 16)
        S.fence(nc.sync)
    return nc


def build_dq():
    nc = bass.Bass("TRN2", target_bir_lowering=False)
    pk1 = nc.dram_tensor("pk1", [H1 * 4, 64], I32, kind="ExternalInput").ap()
    s1s = nc.dram_tensor("s1s", [H1, 8], F32, kind="ExternalInput").ap()
    pk3 = nc.dram_tensor("pk3", [H1 * 4, 64], I32, kind="ExternalInput").ap()
    s3s = nc.dram_tensor("s3s", [H1, 8], F32, kind="ExternalInput").ap()
    pk2 = nc.dram_tensor("pk2", [512 * 86, 64], I32, kind="ExternalInput").ap()
    s2s = nc.dram_tensor("s2s", [512, 172], F32, kind="ExternalInput").ap()
    o1 = nc.dram_tensor("o1", [512, H1], BF16, kind="ExternalOutput").ap()
    o3 = nc.dram_tensor("o3", [512, H1], BF16, kind="ExternalOutput").ap()
    o2 = nc.dram_tensor("o2", [H1, 512], BF16, kind="ExternalOutput").ap()

    from contextlib import ExitStack
    with ExitStack() as st:
        ctx = st.enter_context
        S = Ser(nc)
        st.enter_context(nc.Block())
        idn = make_identity(nc, S, ctx)
        pkt = ctx(nc.sbuf_tensor("pkt", [128, 256], I32))
        s_t = ctx(nc.sbuf_tensor("s_t", [128, 8], F32))
        ms_i = ctx(nc.sbuf_tensor("ms_i", [128, 256], I32))
        ls_i = ctx(nc.sbuf_tensor("ls_i", [128, 256], I32))
        ms_f = ctx(nc.sbuf_tensor("ms_f", [128, 256], F32))
        ls_f = ctx(nc.sbuf_tensor("ls_f", [128, 256], F32))
        dqb = ctx(nc.sbuf_tensor("dqb", [128, 512], BF16))
        ps_t = ctx(nc.psum_tensor("ps_t_dq", [64, 128], BF16))
        stg = ctx(nc.sbuf_tensor("stg", [64, 128], BF16))

        def dq13(pk, ss, oo):
            pkr = pk.rearrange("(hc r j) u -> hc r (j u)", r=128, j=4)
            ssr = ss.rearrange("(hc r) s -> hc r s", r=128)
            for hc in range(HC1):
                S.do(nc.sync, lambda hc=hc: nc.sync.dma_start(pkt[:], pkr[hc]), 16)
                S.do(nc.sync, lambda hc=hc: nc.sync.dma_start(s_t[:], ssr[hc]), 16)
                S.do(nc.vector, lambda: nc.vector.tensor_single_scalar(ms_i[:], pkt[:], 4, ALU.arith_shift_right))
                S.do(nc.vector, lambda: nc.vector.tensor_scalar(ls_i[:], pkt[:], 15, 8, op0=ALU.bitwise_and, op1=ALU.bitwise_xor))
                S.do(nc.vector, lambda: nc.vector.tensor_single_scalar(ls_i[:], ls_i[:], 8, ALU.subtract))
                S.do(nc.vector, lambda: nc.vector.tensor_copy(ms_f[:], ms_i[:]))
                S.do(nc.vector, lambda: nc.vector.tensor_copy(ls_f[:], ls_i[:]))
                for j in range(4):
                    S.do(nc.vector, lambda j=j: nc.vector.tensor_single_scalar(
                        dqb[:, (2 * j) * 64:(2 * j) * 64 + 64], ms_f[:, 64 * j:64 * (j + 1)],
                        s_t[:, 2 * j:2 * j + 1], ALU.mult))
                    S.do(nc.vector, lambda j=j: nc.vector.tensor_single_scalar(
                        dqb[:, (2 * j + 1) * 64:(2 * j + 1) * 64 + 64], ls_f[:, 64 * j:64 * (j + 1)],
                        s_t[:, 2 * j + 1:2 * j + 2], ALU.mult))
                for cl in range(8):
                    S.do(nc.tensor, lambda cl=cl: nc.tensor.transpose(
                        ps_t[:], dqb[:, 64 * cl:64 * (cl + 1)], idn[:]))
                    S.do(nc.vector, lambda: nc.vector.tensor_copy(stg[:], ps_t[:]))
                    S.do(nc.sync, lambda cl=cl, hc=hc, oo=oo: nc.sync.dma_start(
                        oo[64 * cl:64 * (cl + 1), 128 * hc:128 * (hc + 1)], stg[:]), 16)

        dq13(pk1, s1s, o1)
        dq13(pk3, s3s, o3)

        pkt2 = ctx(nc.sbuf_tensor("pkt2", [128, 5504], I32))
        s2t = ctx(nc.sbuf_tensor("s2t", [128, 172], F32))
        m2i = ctx(nc.sbuf_tensor("m2i", [128, 5504], I32))
        m2f = ctx(nc.sbuf_tensor("m2f", [128, 5504], F32))
        l2f = ctx(nc.sbuf_tensor("l2f", [128, 5504], F32))
        dq2 = ctx(nc.sbuf_tensor("dq2", [128, 11008], BF16))
        pk2r = pk2.rearrange("(rc r c) u -> rc r (c u)", r=128, c=86)
        s2r = s2s.rearrange("(rc r) s -> rc r s", r=128)
        for rc in range(4):
            S.do(nc.sync, lambda rc=rc: nc.sync.dma_start(pkt2[:], pk2r[rc]), 16)
            S.do(nc.sync, lambda rc=rc: nc.sync.dma_start(s2t[:], s2r[rc]), 16)
            S.do(nc.vector, lambda: nc.vector.tensor_single_scalar(m2i[:], pkt2[:], 4, ALU.arith_shift_right))
            S.do(nc.vector, lambda: nc.vector.tensor_copy(m2f[:], m2i[:]))
            S.do(nc.vector, lambda: nc.vector.tensor_scalar(m2i[:], pkt2[:], 15, 8, op0=ALU.bitwise_and, op1=ALU.bitwise_xor))
            S.do(nc.vector, lambda: nc.vector.tensor_single_scalar(m2i[:], m2i[:], 8, ALU.subtract))
            S.do(nc.vector, lambda: nc.vector.tensor_copy(l2f[:], m2i[:]))
            for c2 in range(86):
                S.do(nc.vector, lambda c2=c2: nc.vector.tensor_single_scalar(
                    dq2[:, (2 * c2) * 64:(2 * c2) * 64 + 64], m2f[:, 64 * c2:64 * (c2 + 1)],
                    s2t[:, 2 * c2:2 * c2 + 1], ALU.mult))
                S.do(nc.vector, lambda c2=c2: nc.vector.tensor_single_scalar(
                    dq2[:, (2 * c2 + 1) * 64:(2 * c2 + 1) * 64 + 64], l2f[:, 64 * c2:64 * (c2 + 1)],
                    s2t[:, 2 * c2 + 1:2 * c2 + 2], ALU.mult))
            for c in range(172):
                S.do(nc.tensor, lambda c=c: nc.tensor.transpose(
                    ps_t[:], dq2[:, 64 * c:64 * (c + 1)], idn[:]))
                S.do(nc.vector, lambda: nc.vector.tensor_copy(stg[:], ps_t[:]))
                S.do(nc.sync, lambda c=c, rc=rc: nc.sync.dma_start(
                    o2[64 * c:64 * (c + 1), 128 * rc:128 * (rc + 1)], stg[:]), 16)
        S.fence(nc.sync)
    return nc


def build_ffn1():
    nc = bass.Bass("TRN2", target_bir_lowering=False)
    x = nc.dram_tensor("x", [TPC, D], F32, kind="ExternalInput").ap()
    w1t = nc.dram_tensor("w1t", [HC1, 128, 4096], BF16, kind="ExternalInput").ap()
    w3t = nc.dram_tensor("w3t", [HC1, 128, 4096], BF16, kind="ExternalInput").ap()
    w2t = nc.dram_tensor("w2t", [H1, 4096], BF16, kind="ExternalInput").ap()
    xout = nc.dram_tensor("xout", [TPC, D], F32, kind="ExternalOutput").ap()

    from contextlib import ExitStack
    with ExitStack() as st:
        ctx = st.enter_context
        S = Ser(nc)
        st.enter_context(nc.Block())
        idn = make_identity(nc, S, ctx)
        xn = ctx(nc.sbuf_tensor("xn", [128, 32 * TPC], BF16))
        with ExitStack() as stA:
            lns = alloc_ln_scratch(nc, stA.enter_context, "f1")
            emit_ln_transpose(nc, S, x, 0, TPC // 128, xn, 0, idn, lns)

        g_buf = ctx(nc.sbuf_tensor("g_buf", [128, HC1 * 512], BF16))
        w1sb = ctx(nc.sbuf_tensor("w1sb", [128, 4096], BF16))
        w3sb = ctx(nc.sbuf_tensor("w3sb", [128, 4096], BF16))
        a_s = ctx(nc.sbuf_tensor("a_s", [128, 512], BF16))
        b_s = ctx(nc.sbuf_tensor("b_s", [128, 512], BF16))
        w2sb = ctx(nc.sbuf_tensor("w2sb", [128, 512], BF16))
        xr = ctx(nc.sbuf_tensor("xr_f1", [128, 512], F32))
        o_sb = ctx(nc.sbuf_tensor("o_sb_f1", [128, 512], F32))
        psA = ctx(nc.psum_tensor("psA", [128, 512], F32))
        psB = ctx(nc.psum_tensor("psB", [128, 512], F32))
        pso = [ctx(nc.psum_tensor(f"pso_f1{i}", [128, 512], F32)) for i in range(4)]

        for th in range(2):
            for hc in range(HC1):
                S.do(nc.sync, lambda hc=hc: nc.sync.dma_start(w1sb[:], w1t[hc]), 16)
                S.do(nc.sync, lambda hc=hc: nc.sync.dma_start(w3sb[:], w3t[hc]), 16)
                S.chain(nc.tensor, [
                    (lambda dc=dc, th=th: nc.tensor.matmul(
                        psA[:], w1sb[:, 128 * dc:128 * (dc + 1)],
                        xn[:, dc * TPC + th * 512: dc * TPC + th * 512 + 512],
                        start=(dc == 0), stop=(dc == 31)))
                    for dc in range(32)])
                S.chain(nc.tensor, [
                    (lambda dc=dc, th=th: nc.tensor.matmul(
                        psB[:], w3sb[:, 128 * dc:128 * (dc + 1)],
                        xn[:, dc * TPC + th * 512: dc * TPC + th * 512 + 512],
                        start=(dc == 0), stop=(dc == 31)))
                    for dc in range(32)])
                S.do(nc.scalar, lambda: nc.scalar.activation(a_s[:], psA[:], AF.Silu))
                S.do(nc.vector, lambda: nc.vector.tensor_copy(b_s[:], psB[:]))
                S.do(nc.vector, lambda hc=hc: nc.vector.tensor_mul(
                    g_buf[:, hc * 512:(hc + 1) * 512], a_s[:], b_s[:]))
            for ob in range(8):
                for hc in range(HC1):
                    S.do(nc.sync, lambda hc=hc, ob=ob: nc.sync.dma_start(
                        w2sb[:], w2t[128 * hc:128 * (hc + 1), 512 * ob:512 * (ob + 1)]), 16)
                    S.chain(nc.tensor, [
                        (lambda tt=tt, hc=hc: nc.tensor.matmul(
                            pso[tt][:], g_buf[:, hc * 512 + 128 * tt: hc * 512 + 128 * (tt + 1)],
                            w2sb[:], start=(hc == 0), stop=(hc == HC1 - 1), skip_group_check=True))
                        for tt in range(4)])
                for tt in range(4):
                    S.do(nc.sync, lambda tt=tt, th=th, ob=ob: nc.sync.dma_start(
                        xr[:], x[th * 512 + 128 * tt: th * 512 + 128 * (tt + 1), 512 * ob:512 * (ob + 1)]), 16)
                    S.do(nc.vector, lambda tt=tt: nc.vector.scalar_tensor_tensor(
                        o_sb[:], pso[tt][:], 0.5, xr[:], ALU.mult, ALU.add))
                    S.do(nc.sync, lambda tt=tt, th=th, ob=ob: nc.sync.dma_start(
                        xout[th * 512 + 128 * tt: th * 512 + 128 * (tt + 1), 512 * ob:512 * (ob + 1)], o_sb[:]), 16)
        S.fence(nc.sync)
    return nc


_cache = {}


def _get(name, builder):
    if name not in _cache:
        _cache[name] = builder()
    return _cache[name]


def _rope_tables(theta, positions):
    half = 64
    inv = 1.0 / (theta ** (np.arange(half, dtype=np.float64) / half))
    ang = positions[:, None].astype(np.float64) * inv[None, :]
    cos = np.cos(ang).T.astype(np.float32)
    sin = np.sin(ang).T.astype(np.float32)
    return (np.concatenate([cos, cos], 0), np.concatenate([sin, sin], 0))


def _l0_mask():
    BLOCK, LOCAL, VERT = 64, 16, 8
    nb = SEQ // BLOCK
    qb = np.arange(nb)[:, None]
    kb = np.arange(nb)[None, :]
    bm = (kb <= qb) & ((qb - kb < LOCAL) | ((kb + 1) % VERT == 0))
    em = np.kron(bm, np.ones((BLOCK, BLOCK), dtype=bool))
    i = np.arange(SEQ)
    em &= i[:, None] >= i[None, :]
    return em


def _l1_mask():
    i = np.arange(SEQ)
    return (i[None, :] <= i[:, None]) & (i[None, :] > i[:, None] - 1024)


def _run(nc, in_maps, outs):
    res = run_bass_kernel_spmd(nc, in_maps, core_ids=list(range(NCORE)))
    return [[r[o] for o in outs] for r in res.results]


def kernel(L0_attn_hidden_states, L0_attn_w_qkv, L0_attn_b_qkv, L0_attn_w_dense,
           L0_attn_b_dense, L0_ffn_base_W, L0_ffn_gate_W, L0_ffn_lora_A, L0_ffn_lora_B,
           L1_attn_wq, L1_attn_wk, L1_attn_wv, L1_attn_wo,
           L1_ffn_w1, L1_ffn_s1, L1_ffn_w2, L1_ffn_s2, L1_ffn_w3, L1_ffn_s3):
    f32 = np.float32
    xf = np.ascontiguousarray(np.asarray(L0_attn_hidden_states, dtype=f32).reshape(NTOK, D))

    bwT = np.ascontiguousarray(np.asarray(L0_ffn_base_W, f32).T)
    gwT = np.ascontiguousarray(np.asarray(L0_ffn_gate_W, f32).T)
    aaT = np.ascontiguousarray(np.asarray(L0_ffn_lora_A, f32).reshape(224, D).T)
    bal = np.ascontiguousarray(np.asarray(L0_ffn_lora_B, f32).transpose(0, 2, 1).reshape(224, D))
    erep = np.zeros((28, 224), f32)
    for e in range(28):
        erep[e, e * 8:(e + 1) * 8] = 1.0

    wqkv = np.asarray(L0_attn_w_qkv, f32).reshape(8, 6, 128, D)
    w_q0 = wqkv[:, 0:4].reshape(4096, D)
    w_k0 = wqkv[:, 4].reshape(1024, D)
    w_v0 = wqkv[:, 5].reshape(1024, D)
    bqkv = np.asarray(L0_attn_b_qkv, f32).reshape(8, 6, 128)
    b_q0 = np.ascontiguousarray(bqkv[:, 0:4].reshape(32, 128).T)
    b_k0 = np.ascontiguousarray(bqkv[:, 4].reshape(8, 128).T)
    b_v0 = np.ascontiguousarray(bqkv[:, 5].reshape(8, 128).T)
    wqT0 = np.ascontiguousarray(w_q0.T)
    wkT0 = np.ascontiguousarray(w_k0.T)
    wvT0 = np.ascontiguousarray(w_v0.T)
    woT0 = np.ascontiguousarray(np.asarray(L0_attn_w_dense, f32).T)
    bo0 = np.asarray(L0_attn_b_dense, f32)

    wqT1 = np.ascontiguousarray(np.asarray(L1_attn_wq, f32))
    wkT1 = np.ascontiguousarray(np.asarray(L1_attn_wk, f32))
    wvT1 = np.ascontiguousarray(np.asarray(L1_attn_wv, f32))
    woT1 = np.ascontiguousarray(np.asarray(L1_attn_wo, f32))
    zb_q = np.zeros((128, 32), f32)
    zb_kv = np.zeros((128, 8), f32)

    w1p = np.asarray(L1_ffn_w1, np.int32).reshape(H1, 32, 64)
    w3p = np.asarray(L1_ffn_w3, np.int32).reshape(H1, 32, 64)
    w2p = np.asarray(L1_ffn_w2, np.int32).reshape(D, 86, 64)
    s1r = np.asarray(L1_ffn_s1, f32).reshape(H1, 64)
    s3r = np.asarray(L1_ffn_s3, f32).reshape(H1, 64)
    s2r = np.asarray(L1_ffn_s2, f32).reshape(D, 172)

    dq_maps = []
    for c in range(NCORE):
        dq_maps.append({
            "pk1": np.ascontiguousarray(w1p[:, 4 * c:4 * (c + 1), :].reshape(-1, 64)),
            "s1s": np.ascontiguousarray(s1r[:, 8 * c:8 * (c + 1)]),
            "pk3": np.ascontiguousarray(w3p[:, 4 * c:4 * (c + 1), :].reshape(-1, 64)),
            "s3s": np.ascontiguousarray(s3r[:, 8 * c:8 * (c + 1)]),
            "pk2": np.ascontiguousarray(w2p[512 * c:512 * (c + 1)].reshape(-1, 64)),
            "s2s": np.ascontiguousarray(s2r[512 * c:512 * (c + 1)]),
        })
    dq_res = _run(_get("dq", build_dq), dq_maps, ["o1", "o3", "o2"])
    W1T = np.concatenate([r[0] for r in dq_res], axis=0)
    W3T = np.concatenate([r[1] for r in dq_res], axis=0)
    W2T = np.concatenate([r[2] for r in dq_res], axis=1)

    def tile13(W):
        return np.ascontiguousarray(
            W.reshape(32, 128, HC1, 128).transpose(2, 1, 0, 3).reshape(HC1, 128, 4096))

    w1t_t = tile13(W1T)
    w3t_t = tile13(W3T)
    w2t_c = np.ascontiguousarray(W2T)

    pos = np.arange(SEQ)
    m0 = _l0_mask()
    m1 = _l1_mask()
    cosq0 = {}
    attn_meta = {}
    for c in range(NCORE):
        hb = c % 2
        own = np.arange(1024 * hb, 1024 * (hb + 1))
        other = np.arange(1024 * (1 - hb), 1024 * (2 - hb))
        rolled = np.concatenate([own, other])
        attn_meta[c] = (own, rolled)

    def attn_maps(x_cur, wq, wk, wv, wo, bq_, bk_, bv_, bo_, theta, mask):
        maps = []
        for c in range(NCORE):
            b = c // 2
            own, rolled = attn_meta[c]
            xb = xf_view(x_cur, b)
            cq, sq = _rope_tables(theta, own)
            ck, sk = _rope_tables(theta, rolled)
            mT = np.where(mask[np.ix_(own, rolled)], 0.0, NEG).astype(f32).T
            maps.append({
                "x": np.ascontiguousarray(xb[rolled]),
                "wqT": wq, "wkT": wk, "wvT": wv, "woT": wo,
                "bq": bq_, "bk": bk_, "bv": bv_, "bo": bo_,
                "cosq": np.ascontiguousarray(cq), "sinq": np.ascontiguousarray(sq),
                "cosk": np.ascontiguousarray(ck), "sink": np.ascontiguousarray(sk),
                "maskT": np.ascontiguousarray(mT),
            })
        return maps

    def xf_view(x_cur, b):
        return x_cur.reshape(B, SEQ, D)[b]

    def run_ffn0(x_cur):
        maps = [{"x": np.ascontiguousarray(x_cur[TPC * c:TPC * (c + 1)]),
                 "bwT": bwT, "gwT": gwT, "aaT": aaT, "bal": bal, "erep": erep}
                for c in range(NCORE)]
        res = _run(_get("ffn0", build_ffn0), maps, ["xout"])
        return np.concatenate([r[0] for r in res], axis=0)

    def run_ffn1(x_cur):
        maps = [{"x": np.ascontiguousarray(x_cur[TPC * c:TPC * (c + 1)]),
                 "w1t": w1t_t, "w3t": w3t_t, "w2t": w2t_c}
                for c in range(NCORE)]
        res = _run(_get("ffn1", build_ffn1), maps, ["xout"])
        return np.concatenate([r[0] for r in res], axis=0)

    def run_attn(x_cur, layer):
        if layer == 0:
            maps = attn_maps(x_cur, wqT0, wkT0, wvT0, woT0, b_q0, b_k0, b_v0, bo0,
                             1000000.0, m0)
            nc = _get("attn0", lambda: build_attn(1.0 / 128.0, "a0"))
        else:
            maps = attn_maps(x_cur, wqT1, wkT1, wvT1, woT1, zb_q, zb_kv, zb_kv,
                             np.zeros(4096, f32), 10000.0, m1)
            nc = _get("attn1", lambda: build_attn(128.0 ** -0.5, "a1"))
        res = _run(nc, maps, ["xout"])
        out = np.empty_like(x_cur)
        for c in range(NCORE):
            own, _ = attn_meta[c]
            b = c // 2
            out.reshape(B, SEQ, D)[b][own] = res[c][0]
        return out

    x1 = run_ffn0(xf)
    x2 = run_attn(x1, 0)
    x3 = run_ffn0(x2)
    x4 = run_ffn1(x3)
    x5 = run_attn(x4, 1)
    x6 = run_ffn1(x5)
    return x6.reshape(B, SEQ, D)
